# version 08
# baseline (speedup 1.0000x reference)
import math
import contextlib
import numpy as np
import concourse.bass as bass
import concourse.mybir as mybir
from concourse.bass_utils import run_bass_kernel_spmd

F32 = mybir.dt.float32
BF16 = mybir.dt.bfloat16
AF = mybir.ActivationFunctionType
ALU = mybir.AluOpType
AX = mybir.AxisListType

N_CORES = 8
TOK = 1024
NTT = 8
D = 4096
NDC = 32
SLAB = 256
NSLAB = D // SLAB
HEADS = 32
ER = 256
NEG = -30000.0
QSCALE = 1.0 / math.sqrt(128.0)
EPS = 1e-5

ENGS = ("sync", "gpsimd", "vector", "scalar", "tensor")
STAGE = 6
DBG = False


class Prog:

    def __init__(self, nc, stack):
        self.nc = nc
        self.steps = {e: [] for e in ENGS}
        self.cnt = {}
        self.h = {}
        self.lastw = {e: {} for e in ENGS}
        self.stack = stack
        self.nbar = 0

    def sem(self, name):
        self.h[name] = self.stack.enter_context(self.nc.semaphore(name))
        self.cnt[name] = 0
        return name

    def op(self, eng, fn, sem=None, inc=1):
        if sem is not None:
            h = self.h[sem]
            self.steps[eng].append(
                lambda fn=fn, h=h, inc=inc: fn().then_inc(h, inc))
            self.cnt[sem] += inc
            return (sem, self.cnt[sem])
        self.steps[eng].append(fn)
        return None

    def dma(self, eng, out, in_, sem, **kw):
        e = getattr(self.nc, eng)
        return self.op(eng, lambda e=e, out=out, in_=in_, kw=kw:
                       e.dma_start(out=out, in_=in_, **kw), sem, 16)

    def wait(self, eng, tok):
        if tok is None:
            return
        sem, val = tok
        if val <= 0 or self.lastw[eng].get(sem, -1) >= val:
            return
        self.lastw[eng][sem] = val
        h = self.h[sem]
        e = getattr(self.nc, eng)
        self.steps[eng].append(lambda e=e, h=h, val=val: e.wait_ge(h, val))

    def now(self, sem):
        return (sem, self.cnt[sem])

    def barrier(self):
        name = f"bar{self.nbar}"
        self.nbar += 1
        self.sem(name)
        h = self.h[name]
        for e in ENGS:
            eng = getattr(self.nc, e)
            self.steps[e].append(lambda eng=eng, h=h: eng.nop().then_inc(h, 1))
        self.cnt[name] = len(ENGS)
        for e in ENGS:
            self.wait(e, (name, len(ENGS)))

    def emit(self, block):
        def mk(e):
            def section(_):
                for fn in self.steps[e]:
                    fn()
            return section
        block.sync(mk("sync"))
        block.gpsimd(mk("gpsimd"))
        block.vector(mk("vector"))
        block.scalar(mk("scalar"))
        block.tensor(mk("tensor"))


class PsumPool:

    def __init__(self, nc, stack, name, n, shape, dtype=F32):
        self.t = [stack.enter_context(nc.psum_tensor(f"{name}{i}", shape, dtype))
                  for i in range(n)]
        self.cons = [None] * n

    def get(self, P, i):
        P.wait("tensor", self.cons[i])
        return self.t[i]


def build_program(stage=STAGE):
    nc = bass.Bass("TRN2", target_bir_lowering=False)

    def din(name, shape):
        return nc.dram_tensor(name, shape, F32, kind="ExternalInput").ap()

    x_d = din("x_shard", [TOK, D])
    wT_d = din("wT", [D, D])
    wq_d = din("wq", [D, D])
    wo_d = din("wo", [D, D])
    wkv_d = din("wkv", [D, 256])
    aT_d = din("aT", [D, ER])
    bfp_d = din("bfp", [ER, D])
    bias_d = din("bias", [1, D])
    coef_d = din("coeffs", [1, 16])
    scal_d = din("scales", [1, 16])
    mask_d = din("mask8", [128, 1024])
    out_d = nc.dram_tensor("out_shard", [TOK, D], F32, kind="ExternalOutput").ap()
    dbg_t = {}
    if DBG:
        for nm, shape in [("d_x1", [128, D]), ("d_ln1", [128, 512 * 4]),
                          ("d_xaT", [128, 1024]), ("d_q", [128, D]),
                          ("d_kfm", [128, 1024]), ("d_v", [128, 1024]),
                          ("d_attn", [128, 1024]), ("d_x2", [128, D]),
                          ("d_den", [128, 32]), ("d_wexp", [128, 2]),
                          ("d_stats", [128, 8]), ("d_x2f", [TOK, D])]:
            dbg_t[nm] = nc.dram_tensor(nm, shape, F32, kind="ExternalOutput").ap()

    wT_bf = nc.dram_tensor("wT_bf", [NSLAB, NDC, 128, SLAB], BF16).ap()
    wq_bf = nc.dram_tensor("wq_bf", [NSLAB, NDC, 128, SLAB], BF16).ap()
    wo_bf = nc.dram_tensor("wo_bf", [NSLAB, NDC, 128, SLAB], BF16).ap()
    aT_bf = nc.dram_tensor("aT_bf", [NDC, 128, ER], BF16).ap()
    bfp_bf = nc.dram_tensor("bfp_bf", [2, 128, D], BF16).ap()
    wkv_bf = nc.dram_tensor("wkv_bf", [NDC, 128, 256], BF16).ap()
    x1_d = nc.dram_tensor("x1_save", [NTT, 128, D], F32).ap()
    kv_in = nc.dram_tensor("kv_in", [TOK, 256], BF16)
    kv_out = nc.dram_tensor("kv_out", [N_CORES * TOK, 256], BF16)

    with contextlib.ExitStack() as outer:
        P = Prog(nc, outer)
        for s in ["dw", "dg", "dst", "pe", "act", "dve", "gp", "cc"]:
            P.sem(s)

        def sb(st, name, shape, dtype):
            return st.enter_context(nc.sbuf_tensor(name, shape, dtype))

        tapbuf = sb(outer, "tapbuf", [128, D], F32) if DBG else None

        def tap(nm, src_ap, after=None):
            if not DBG:
                return
            ncols = src_ap.shape[-1]
            P.wait("vector", after)
            P.wait("vector", P.now("dst"))
            P.op("vector", lambda src_ap=src_ap, ncols=ncols:
                 nc.vector.tensor_copy(tapbuf[:, 0:ncols], src_ap), "dve")
            P.wait("sync", P.now("dve"))
            P.dma("sync", dbg_t[nm][:, :], tapbuf[:, 0:ncols], "dst")

        U1 = sb(outer, "U1", [128, 32768], BF16)
        U2 = sb(outer, "U2", [128, 16384], BF16)
        idb_sb = sb(outer, "idb", [128, 128], BF16)
        halfcol_sb = sb(outer, "halfcol", [1, 128], BF16)
        eps_sb = sb(outer, "eps", [128, 1], F32)
        fence_sb = sb(outer, "fence", [128, 1], F32)
        bias_sb = sb(outer, "biasr", [1, D], BF16)
        wexp_sb = sb(outer, "wexp", [128, 2], F32)
        U1f = U1.ap().bitcast(F32)

        prep = outer.enter_context(contextlib.ExitStack())
        onesb_sb = sb(prep, "onesb", [128, 128], BF16)
        sel_sb = sb(prep, "sel", [16, 256], F32)
        wrow_sb = sb(prep, "wrow", [1, 16], F32)
        srow_sb = sb(prep, "srow", [1, 16], F32)
        idf16_sb = sb(prep, "idf16", [16, 16], F32)
        wcol_sb = sb(prep, "wcol", [16, 1], F32)
        wcol_ps = prep.enter_context(nc.psum_tensor("wcolps", [16, 16], F32))
        wexp_ps = prep.enter_context(nc.psum_tensor("wexpps", [128, 2], F32))

        P.op("gpsimd", lambda: nc.gpsimd.memset(onesb_sb[:], 1.0), "gp")
        P.op("gpsimd", lambda: nc.gpsimd.affine_select(
            idb_sb[:], onesb_sb[:], pattern=[[1, 128]], base=0,
            channel_multiplier=-1, compare_op=ALU.is_equal, fill=0.0), "gp")
        P.op("gpsimd", lambda: nc.gpsimd.memset(halfcol_sb[:], 0.5), "gp")
        P.op("gpsimd", lambda: nc.gpsimd.memset(eps_sb[:], EPS), "gp")
        P.op("gpsimd", lambda: nc.gpsimd.memset(sel_sb[:], 1.0), "gp")
        P.op("gpsimd", lambda: nc.gpsimd.affine_select(
            sel_sb[:], sel_sb[:], pattern=[[1, 256]], base=0,
            channel_multiplier=-16, compare_op=ALU.is_ge, fill=0.0), "gp")
        P.op("gpsimd", lambda: nc.gpsimd.affine_select(
            sel_sb[:], sel_sb[:], pattern=[[-1, 256]], base=15,
            channel_multiplier=16, compare_op=ALU.is_ge, fill=0.0), "gp")
        P.op("gpsimd", lambda: nc.gpsimd.memset(idf16_sb[:], 1.0), "gp")
        P.op("gpsimd", lambda: nc.gpsimd.affine_select(
            idf16_sb[:], idf16_sb[:], pattern=[[1, 16]], base=0,
            channel_multiplier=-1, compare_op=ALU.is_equal, fill=0.0), "gp")
        v_gp_prep = P.now("gp")

        P.dma("sync", wrow_sb[:], coef_d[:, :], "dw")
        P.dma("sync", srow_sb[:], scal_d[:, :], "dw")
        v_cs = P.now("dw")
        P.wait("vector", v_cs)
        P.op("vector", lambda: nc.vector.tensor_tensor(
            srow_sb[:], wrow_sb[:], srow_sb[:], ALU.mult), "dve")
        v_wrow = P.now("dve")
        P.wait("tensor", v_wrow)
        P.wait("tensor", v_gp_prep)
        P.op("tensor", lambda: nc.tensor.transpose(
            wcol_ps[:, 0:1], srow_sb[:], idf16_sb[0:1, 0:1]), "pe")
        v1 = P.now("pe")
        P.wait("vector", v1)
        P.op("vector", lambda: nc.vector.tensor_copy(
            wcol_sb[:], wcol_ps[:, 0:1]), "dve")
        v2 = P.now("dve")
        P.wait("tensor", v2)
        for erc in range(2):
            P.op("tensor", lambda erc=erc: nc.tensor.matmul(
                wexp_ps[:, erc:erc + 1], sel_sb[:, 128 * erc:128 * (erc + 1)],
                wcol_sb[:], start=True, stop=True), "pe")
        v3 = P.now("pe")
        P.wait("vector", v3)
        P.op("vector", lambda: nc.vector.tensor_copy(wexp_sb[:], wexp_ps[:]),
             "dve")
        v_wexp = P.now("dve")

        P.dma("gpsimd", bias_sb[:], bias_d[:, :], "dg")
        v_bias = P.now("dg")

        cast_done = {}
        cast_slab = {}
        P.dma("gpsimd", aT_bf[:], aT_d.rearrange("(dc p) c -> dc p c", p=128),
              "dg")
        cast_done["aT"] = P.now("dg")
        P.dma("gpsimd", bfp_bf[:], bfp_d.rearrange("(e p) c -> e p c", p=128),
              "dg")
        cast_done["bfp"] = P.now("dg")
        P.dma("gpsimd", wkv_bf[:], wkv_d.rearrange("(dc p) c -> dc p c", p=128),
              "dg")
        cast_done["wkv"] = P.now("dg")
        for nm, src, dst in [("wT", wT_d, wT_bf)]:
            cast_slab[nm] = []
            for s in range(NSLAB):
                P.dma("gpsimd", dst[s],
                      src[:, s * SLAB:(s + 1) * SLAB].rearrange(
                          "(dc p) c -> dc p c", p=128), "dg")
                cast_slab[nm].append(P.now("dg"))
            cast_done[nm] = P.now("dg")

        def dve_gap():
            P.op("vector", lambda: nc.vector.tensor_copy(
                fence_sb[:], eps_sb[:]), "dve")

        def act_fence():
            return P.op("scalar", lambda: nc.scalar.copy(
                fence_sb[:], eps_sb[:]), "act")

        def make_stats(st, tag):
            return [dict(
                s1=sb(st, f"{tag}s1{i}", [128, 1], F32),
                s2=sb(st, f"{tag}s2{i}", [128, 1], F32),
                var=sb(st, f"{tag}var{i}", [128, 1], F32),
                std=sb(st, f"{tag}std{i}", [128, 1], F32),
                lntm=sb(st, f"{tag}lntm{i}", [128, D], BF16),
                cons_dve=None, cons_pe=None)
                for i in range(1 if DBG else 2)]

        def emit_ln(x_ap, x_ready, ti, ln_h, half_scale, stats2, lnpool,
                    lh_guard=None, dbg_snap=False):
            stats = stats2[ti % len(stats2)]
            s1, s2, var, std, lntm = (stats[k] for k in
                                      ("s1", "s2", "var", "std", "lntm"))
            P.wait("scalar", stats["cons_dve"])
            P.wait("scalar", stats["cons_pe"])
            P.wait("scalar", x_ready)
            P.op("scalar", lambda: nc.scalar.square(lntm[:], x_ap), "act")
            P.wait("vector", P.now("act"))
            P.wait("vector", x_ready)
            P.op("vector", lambda: nc.vector.reduce_sum(
                s1[:], x_ap, axis=AX.X), "dve")
            P.op("vector", lambda: nc.vector.reduce_sum(
                s2[:], lntm[:], axis=AX.X), "dve")
            dve_gap()
            P.op("vector", lambda: nc.vector.tensor_scalar(
                s1[:], s1[:], 1.0 / D, None, ALU.mult), "dve")
            P.op("vector", lambda: nc.vector.tensor_scalar(
                s2[:], s2[:], 1.0 / D, None, ALU.mult), "dve")
            dve_gap()
            P.op("vector", lambda: nc.vector.tensor_tensor(
                var[:], s1[:], s1[:], ALU.mult), "dve")
            dve_gap()
            P.op("vector", lambda: nc.vector.tensor_tensor(
                std[:], s2[:], var[:], ALU.subtract), "dve")
            dve_gap()
            P.wait("scalar", P.now("dve"))
            P.op("scalar", lambda: nc.scalar.activation(
                var[:], std[:], AF.Sqrt, bias=eps_sb[:]), "act")
            act_fence()
            P.wait("vector", P.now("act"))
            P.op("vector", lambda: nc.vector.reciprocal(std[:], var[:]), "dve")
            dve_gap()
            if half_scale:
                P.op("vector", lambda: nc.vector.tensor_scalar(
                    std[:], std[:], 0.5, None, ALU.mult), "dve")
                dve_gap()
            if dbg_snap and DBG:
                for ci, t_ in enumerate((s1, s2, var, std)):
                    P.op("vector", lambda t_=t_, ci=ci: nc.vector.tensor_copy(
                        tapbuf[:, 3000 + ci:3001 + ci], t_[:]), "dve")
            P.op("vector", lambda: nc.vector.tensor_scalar(
                lntm[:], x_ap, s1[:], std[:], ALU.subtract, ALU.mult), "dve")
            if dbg_snap and DBG:
                P.op("vector", lambda: nc.vector.tensor_copy(
                    tapbuf[:, 3004:3008], lntm[:, 0:4]), "dve")
                P.wait("sync", P.now("dve"))
                P.dma("sync", dbg_t["d_stats"][:, :], tapbuf[:, 3000:3008],
                      "dst")
            stats["cons_dve"] = P.now("dve")
            P.wait("tensor", P.now("dve"))
            P.wait("vector", lh_guard)
            last = None
            for grp in range(4):
                bank = lnpool.get(P, grp % 2)
                for j in range(8):
                    dc = grp * 8 + j
                    P.op("tensor", lambda dc=dc, bank=bank, j=j:
                         nc.tensor.matmul(
                             bank[:, 128 * j:128 * (j + 1)],
                             lntm[:, 128 * dc:128 * (dc + 1)],
                             idb_sb[:], is_transpose=True, start=(j == 0),
                             stop=(j == 7), skip_group_check=True), "pe")
                P.wait("vector", P.now("pe"))
                dst = ln_h.rearrange("p (dc t) -> p dc t", t=512)[
                    :, grp * 8:(grp + 1) * 8, ti * 128:(ti + 1) * 128]
                P.op("vector", lambda bank=bank, dst=dst:
                     nc.vector.tensor_copy(
                         dst, bank[:].rearrange("p (a b) -> p a b", b=128)),
                     "dve")
                last = P.now("dve")
                lnpool.cons[grp % 2] = last
            stats["cons_pe"] = P.now("pe")
            return last

        def emit_proj(name, lhs_h, lhs_ready, w_dram, w_cast_v, tts,
                      pool, evac_fn, extras=None, pre_slab=None):
            wbuf = [U2.ap()[:, i * 8192:(i + 1) * 8192] for i in range(2)]
            free_v = [P.now("pe"), P.now("pe")]
            direct = w_cast_v == "direct"
            for si in range(NSLAB):
                s = si
                par = si % 2
                if direct:
                    P.wait("gpsimd", free_v[par])
                    P.dma("gpsimd",
                          wbuf[par].rearrange("p (dc c) -> p dc c", c=SLAB),
                          w_dram[:, s * SLAB:(s + 1) * SLAB].rearrange(
                              "(dc p) c -> p dc c", p=128), "dg")
                    v_load = P.now("dg")
                else:
                    P.wait("sync", w_cast_v[s]
                           if isinstance(w_cast_v, list) else w_cast_v)
                    P.wait("sync", free_v[par])
                    P.dma("sync",
                          wbuf[par].rearrange("p (dc c) -> p dc c", c=SLAB),
                          w_dram[s].rearrange("dc p c -> p dc c"), "dw")
                    v_load = P.now("dw")
                extra_tok = pre_slab(s, par) if pre_slab is not None else None
                P.wait("tensor", v_load)
                P.wait("tensor", extra_tok)
                P.wait("tensor", lhs_ready)
                for ti in range(len(tts)):
                    psum = pool.get(P, ti)
                    for dc in range(NDC):
                        is_last = (dc == NDC - 1) and extras is None
                        P.op("tensor",
                             lambda dc=dc, ti=ti, par=par, psum=psum,
                             is_last=is_last:
                             nc.tensor.matmul(
                                 psum[:, 0:SLAB],
                                 lhs_h[:, dc * 512 + ti * 128:
                                       dc * 512 + ti * 128 + 128],
                                 wbuf[par][:, dc * SLAB:(dc + 1) * SLAB],
                                 start=(dc == 0), stop=is_last), "pe")
                    if extras is not None:
                        extras(s, ti, psum, par)
                    v_pe = P.now("pe")
                    pool.cons[ti] = evac_fn(ti, s, psum, v_pe)
                free_v[par] = P.now("pe")

        atb_cons = [None, None]

        def emit_xa(st_atb, xaT, xapool, ln_h, v_ln, xa_guard=None):
            for erc in range(2):
                for g in range(8):
                    gpar = g % 2
                    P.wait("sync", cast_done["aT"])
                    P.wait("sync", atb_cons[gpar])
                    P.dma("sync",
                          st_atb[:, gpar * 1024:gpar * 1024 + 1024].rearrange(
                              "p (a c) -> p a c", c=ER),
                          aT_bf[g * 4:(g + 1) * 4].rearrange(
                              "a p c -> p a c"), "dw")
                    v_atb = P.now("dw")
                    P.wait("tensor", v_atb)
                    P.wait("tensor", v_ln)
                    if g == 0:
                        P.wait("tensor", xapool.cons[erc])
                    for j in range(4):
                        dc = g * 4 + j
                        P.op("tensor", lambda erc=erc, dc=dc, j=j, gpar=gpar:
                             nc.tensor.matmul(
                                 xapool.t[erc][:],
                                 st_atb[:, gpar * 1024 + j * ER + erc * 128:
                                        gpar * 1024 + j * ER + erc * 128 + 128],
                                 ln_h[:, dc * 512:dc * 512 + 512],
                                 start=(dc == 0), stop=(dc == NDC - 1)), "pe")
                    atb_cons[gpar] = P.now("pe")
            v_xaps = P.now("pe")
            P.wait("vector", v_xaps)
            P.wait("vector", xa_guard)
            P.wait("vector", v_wexp)
            for erc in range(2):
                P.op("vector", lambda erc=erc: nc.vector.tensor_scalar(
                    xaT[:, erc * 512:(erc + 1) * 512], xapool.t[erc][:],
                    wexp_sb[:, erc:erc + 1], None, ALU.mult), "dve")
                xapool.cons[erc] = P.now("dve")
            return P.now("dve")

        def mk_ffn_extras(xaT, bfwb, free_bfw, get_vxa):
            def extras(s, ti, psum, par):
                P.wait("tensor", get_vxa())
                for erc in range(2):
                    P.op("tensor", lambda erc=erc, ti=ti, par=par, psum=psum:
                         nc.tensor.matmul(
                             psum[:, 0:SLAB],
                             xaT[:, erc * 512 + ti * 128:
                                 erc * 512 + ti * 128 + 128],
                             bfwb[:, par * 512 + erc * 256:
                                  par * 512 + erc * 256 + 256],
                             start=False, stop=False), "pe")
                P.wait("tensor", v_bias)
                P.wait("tensor", v_gp_prep)
                P.op("tensor", lambda s=s, psum=psum: nc.tensor.matmul(
                    psum[:, 0:SLAB], halfcol_sb[:],
                    bias_sb[:, s * SLAB:(s + 1) * SLAB],
                    start=False, stop=True), "pe")
                free_bfw[par] = P.now("pe")
            return extras

        def mk_pre_slab(bfwb, free_bfw):
            if free_bfw[0] is None:
                free_bfw[0] = P.now("pe")
                free_bfw[1] = P.now("pe")

            def pre_slab(s, par):
                P.wait("sync", cast_done["bfp"])
                P.wait("sync", free_bfw[par])
                for erc in range(2):
                    P.dma("sync",
                          bfwb[:, par * 512 + erc * 256:
                               par * 512 + erc * 256 + 256],
                          bfp_bf[erc][:, s * SLAB:(s + 1) * SLAB], "dw")
                return P.now("dw")
            return pre_slab

        P.barrier()
        prep.close()
        x1_store_v = None
        if stage >= 1:
            p1 = outer.enter_context(contextlib.ExitStack())
            ln1_h = sb(p1, "ln1_h", [128, NDC * 512], BF16)
            stats = make_stats(p1, "st1")
            atb = sb(p1, "atb1", [128, 2048], BF16)
            xaT = sb(p1, "xaT1", [128, 1024], BF16)
            bfwb = sb(p1, "bfwb1", [128, 1024], BF16)
            lnpool = PsumPool(nc, p1, "lnps1", 2, [128, 1024], BF16)
            xapool = PsumPool(nc, p1, "xaps1", 2, [128, 512])
            pool = PsumPool(nc, p1, "ffn1ps", 4, [128, 512])
            xh = [U1f[:, i * D:(i + 1) * D] for i in range(4)]

            for half in range(2):
                tts = [half * 4 + i for i in range(4)]
                xv = []
                for i, tokt in enumerate(tts):
                    P.wait("sync", x1_store_v)
                    P.dma("sync", xh[i],
                          x_d.rearrange("(tt p) d -> tt p d", p=128)[tokt],
                          "dw")
                    xv.append(P.now("dw"))
                g_pe = P.now("pe")
                v_ln1 = None
                for i in range(4):
                    v_ln1 = emit_ln(xh[i], xv[i], i, ln1_h[:], True,
                                    stats, lnpool, lh_guard=g_pe,
                                    dbg_snap=(half == 0 and i == 0))
                v_xa = emit_xa(atb, xaT, xapool, ln1_h[:], v_ln1, xa_guard=g_pe)
                free_bfw = [None, None]

                def ffn_evac(ti, s, psum, v_pe, xv=xv):
                    P.wait("vector", v_pe)
                    P.wait("vector", xv[ti])
                    P.op("vector", lambda ti=ti, s=s, psum=psum:
                         nc.vector.tensor_tensor(
                             xh[ti][:, s * SLAB:(s + 1) * SLAB],
                             xh[ti][:, s * SLAB:(s + 1) * SLAB],
                             psum[:, 0:SLAB], ALU.add), "dve")
                    return P.now("dve")

                emit_proj(f"ffn1h{half}", ln1_h[:], v_ln1, wT_bf,
                          cast_slab["wT"], tts, pool, ffn_evac,
                          extras=mk_ffn_extras(xaT, bfwb, free_bfw,
                                               lambda v=v_xa: v),
                          pre_slab=mk_pre_slab(bfwb, free_bfw))

                for i, tokt in enumerate(tts):
                    P.wait("sync", pool.cons[i])
                    P.dma("sync", x1_d[tokt], xh[i], "dst")
                x1_store_v = P.now("dst")
                if half == 0:
                    tap("d_x1", xh[0], pool.cons[0])
                    tap("d_ln1", ln1_h[:, 0:2048], v_ln1)
                    tap("d_xaT", xaT[:], v_xa)
                    tap("d_wexp", wexp_sb[:], v_wexp)
            P.barrier()
            p1.close()

        s23 = outer.enter_context(contextlib.ExitStack())
        q_tm = U1.ap()
        v_cc = None
        if stage >= 2:
            p2 = s23.enter_context(contextlib.ExitStack())
            ln2_h = sb(p2, "ln2_h", [128, NDC * 512], BF16)
            stats = make_stats(p2, "st2")
            nx1t = 1 if DBG else 2
            x1t = sb(p2, "x1t", [128, D * 2 * nx1t], BF16)
            x1tf = x1t.ap().bitcast(F32)
            kvw = sb(p2, "kvw", [128, NDC * 256], BF16)
            kvt = sb(p2, "kvt", [128, 512], BF16)
            lnpool = PsumPool(nc, p2, "lnps2", 2, [128, 1024], BF16)
            qpool = PsumPool(nc, p2, "qps", 4, [128, 512])
            kvpool = PsumPool(nc, p2, "kvps", 2, [128, 512])

            P.wait("sync", cast_done["wkv"])
            P.dma("sync", kvw[:].rearrange("p (dc c) -> p dc c", c=256),
                  wkv_bf[:].rearrange("dc p c -> p dc c"), "dw")
            v_kvw = P.now("dw")
            kvt_free = [None, None]
            for half in range(2):
                tts = [half * 4 + i for i in range(4)]
                g_pe2 = P.now("pe")
                v_ln2 = None
                for i, tokt in enumerate(tts):
                    st_i = stats[i % len(stats)]
                    P.wait("sync", st_i["cons_dve"])
                    P.wait("sync", st_i["cons_pe"])
                    if nx1t == 1:
                        P.wait("sync", P.now("act"))
                        P.wait("sync", P.now("dve"))
                    P.wait("sync", x1_store_v)
                    xsl = x1tf[:, (i % nx1t) * D:(i % nx1t) * D + D]
                    P.dma("sync", xsl, x1_d[tokt], "dw")
                    v_x1 = P.now("dw")
                    v_ln2 = emit_ln(xsl, v_x1, i, ln2_h[:], False,
                                    stats, lnpool, lh_guard=g_pe2)

                P.wait("tensor", v_kvw)
                for ti in range(4):
                    tokt = half * 4 + ti
                    par = ti % 2
                    psum = kvpool.get(P, par)
                    for dc in range(NDC):
                        P.op("tensor", lambda dc=dc, ti=ti, psum=psum:
                             nc.tensor.matmul(
                                 psum[:, 0:256],
                                 ln2_h[:, dc * 512 + ti * 128:
                                       dc * 512 + ti * 128 + 128],
                                 kvw[:, dc * 256:(dc + 1) * 256],
                                 start=(dc == 0), stop=(dc == NDC - 1)), "pe")
                    v_pe = P.now("pe")
                    P.wait("vector", v_pe)
                    P.wait("vector", kvt_free[par])
                    P.op("vector", lambda par=par, psum=psum:
                         nc.vector.tensor_copy(
                             kvt[:, par * 256:(par + 1) * 256],
                             psum[:, 0:256]), "dve")
                    dve_gap()
                    v_e = P.now("dve")
                    kvpool.cons[par] = v_e
                    P.wait("gpsimd", v_e)
                    P.dma("gpsimd",
                          kv_in.ap().rearrange("(tt p) c -> tt p c", p=128)[tokt],
                          kvt[:, par * 256:(par + 1) * 256], "dg")
                    kvt_free[par] = P.now("dg")
                if half == 1:
                    v_kv_dma = P.now("dg")
                    P.wait("gpsimd", v_kv_dma)
                    P.op("gpsimd", lambda: nc.gpsimd.collective_compute(
                        "AllGather", ALU.bypass,
                        replica_groups=[list(range(N_CORES))],
                        ins=[kv_in.ap().opt()], outs=[kv_out.ap().opt()]),
                        "cc")
                    v_cc = P.now("cc")
                def q_evac(ti, s, psum, v_pe, half=half):
                    tokt = half * 4 + ti
                    P.wait("scalar", v_pe)
                    P.op("scalar", lambda tokt=tokt, s=s, psum=psum:
                         nc.scalar.activation(
                             q_tm[:, tokt * D + s * SLAB:
                                  tokt * D + (s + 1) * SLAB],
                             psum[:, 0:SLAB], AF.Copy, scale=QSCALE), "act")
                    return P.now("act")

                emit_proj(f"qprojh{half}", ln2_h[:], v_ln2, wq_d,
                          "direct", tts, qpool, q_evac)

            tap("d_q", q_tm[:, 0:D].bitcast(F32).bitcast(BF16) if False else q_tm[:, 0:D], P.now("act"))
            P.barrier()
            p2.close()

        attn_d = nc.dram_tensor("attn_save", [NTT, 128, D], BF16).ap()
        if stage >= 3:
            p3 = s23.enter_context(contextlib.ExitStack())
            attn_tm = sb(p3, "attn_tm", [128, 2 * D], BF16)
            k_fm = U2.ap()[:, 0:8192]
            v_sb = U2.ap()[:, 8192:16384]
            mask_sb = sb(p3, "mask", [128, 1024], F32)
            ktmp = sb(p3, "ktmp", [128, 2048], BF16)
            sbnd = sb(p3, "sbnd", [128, 2048], F32)
            qh = sb(p3, "qh", [128, 256], BF16)
            nmx = sb(p3, "nmx", [128, 2], F32)
            den = sb(p3, "den", [128, 32], F32)
            dsum = sb(p3, "dsum", [128, 2], F32)
            denc = sb(p3, "denc", [128, 32], F32)
            drec = sb(p3, "drec", [128, 2], F32)
            probs = sb(p3, "probs", [128, 2048], BF16)
            probsT = sb(p3, "probsT", [128, 2048], BF16)
            kstack = p3.enter_context(contextlib.ExitStack())
            kpool = PsumPool(nc, kstack, "kps", 2, [128, 1024], BF16)

            P.dma("sync", mask_sb[:], mask_d[:, :], "dw")
            v_mask = P.now("dw")

            P.wait("sync", v_cc)
            kv_re3 = kv_out.ap().rearrange("(r m p) c -> r m p c", p=128, m=8)
            ktmp_free = [None, None]
            for g in range(8):
                P.dma("sync", v_sb[:, g * 1024:(g + 1) * 1024].rearrange(
                    "p (t c) -> p t c", c=128),
                    kv_re3[:, g, :, 128:256].rearrange(
                        "t p c -> p t c"), "dw")
                par = g % 2
                P.wait("sync", ktmp_free[par])
                P.dma("sync",
                      ktmp[:, par * 1024:(par + 1) * 1024].rearrange(
                          "p (t c) -> p t c", c=128),
                      kv_re3[:, g, :, 0:128].rearrange(
                          "t p c -> p t c"), "dw")
                v_l = P.now("dw")
                P.wait("tensor", v_l)
                bank = kpool.get(P, par)
                for j in range(8):
                    P.op("tensor", lambda j=j, bank=bank, par=par:
                         nc.tensor.matmul(
                             bank[:, 128 * j:128 * (j + 1)],
                             ktmp[:, par * 1024 + 128 * j:
                                  par * 1024 + 128 * (j + 1)],
                             idb_sb[:], is_transpose=True, start=(j == 0),
                             stop=(j == 7), skip_group_check=True), "pe")
                ktmp_free[par] = P.now("pe")
                P.wait("vector", ktmp_free[par])
                P.op("vector", lambda g=g, bank=bank: nc.vector.tensor_copy(
                    k_fm[:, g * 1024:(g + 1) * 1024], bank[:]), "dve")
                kpool.cons[par] = P.now("dve")
            v_kv_ready = P.now("dve")
            P.wait("tensor", v_kv_ready)
            P.wait("tensor", P.now("dw"))
            kstack.close()
            spool = PsumPool(nc, p3, "sps", 3, [128, 512])
            tpool = PsumPool(nc, p3, "tps", 2, [128, 1024], BF16)
            opool = PsumPool(nc, p3, "ops", 2, [128, 512])
            qapool = PsumPool(nc, p3, "qaps", 1, [128, 1024], BF16)

            probs_cons = [None, None, None, None]
            probsT_cons = [None, None, None, None]
            sbnd_cons = [None, None]
            den_cons = [None, None]
            denc_cons = [None, None]
            nmx_cons = [None, None]
            qh_cons = [None, None]
            attn_dma = [None, None]
            sps_i = [0]
            tps_i = [0]
            pbs_i = [0]
            pts_i = [0]
            pendT = []
            pendPV = []
            pend_epi = []

            def emit_S(c):
                a = sps_i[0] % 3
                sps_i[0] += 1
                c["slot"] = a
                psum = spool.get(P, a)
                P.wait("tensor", c["v_qh"])
                if c["kind"] == "g":
                    k0 = c["g"] * 512
                else:
                    k0 = c["qt"] * 1024 + (0 if c["kind"] == "b0" else 512)
                par = c["par"]
                P.op("tensor", lambda psum=psum, par=par, k0=k0:
                     nc.tensor.matmul(
                         psum[:], qh[:, par * 128:(par + 1) * 128],
                         k_fm[:, k0:k0 + 512], start=True, stop=True), "pe")
                c["v_s"] = P.now("pe")

            def emit_exp(c):
                pb = pbs_i[0] % 4
                pbs_i[0] += 1
                c["pb"] = pb
                par = c["par"]
                P.wait("scalar", probs_cons[pb])
                P.wait("scalar", den_cons[par])
                if c["kind"] == "g":
                    P.wait("scalar", c["v_s"])
                    P.wait("scalar", c["v_nm"])
                    src = spool.t[c["slot"]][:]
                else:
                    P.wait("scalar", c["v_nm"])
                    h2 = 0 if c["kind"] == "b0" else 1
                    src = sbnd[:, par * 1024 + h2 * 512:
                               par * 1024 + h2 * 512 + 512]
                dcol = c["den_col"]
                P.op("scalar", lambda src=src, pb=pb, par=par, dcol=dcol:
                     nc.scalar.activation(
                         probs[:, pb * 512:(pb + 1) * 512], src, AF.Exp,
                         bias=nmx[:, par:par + 1],
                         accum_out=den[:, par * 16 + dcol:
                                       par * 16 + dcol + 1]), "act")
                c["v_exp"] = P.now("act")
                if c["kind"] == "g":
                    spool.cons[c["slot"]] = c["v_exp"]

            def emit_T(c):
                b = tps_i[0] % 2
                tps_i[0] += 1
                bank = tpool.get(P, b)
                P.wait("tensor", c["v_exp"])
                pb = c["pb"]
                for j in range(4):
                    P.op("tensor", lambda j=j, bank=bank, pb=pb:
                         nc.tensor.matmul(
                             bank[:, 128 * j:128 * (j + 1)],
                             probs[:, pb * 512 + 128 * j:
                                   pb * 512 + 128 * (j + 1)],
                             idb_sb[:], is_transpose=True,
                             start=(j == 0), stop=(j == 3),
                             skip_group_check=True), "pe")
                probs_cons[pb] = P.now("pe")
                pt = pts_i[0] % 4
                pts_i[0] += 1
                c["pt"] = pt
                P.wait("vector", P.now("pe"))
                P.wait("vector", probsT_cons[pt])
                P.op("vector", lambda b=b, pt=pt: nc.vector.tensor_copy(
                    probsT[:, pt * 512:(pt + 1) * 512],
                    tpool.t[b][:, 0:512]), "dve")
                tpool.cons[b] = P.now("dve")
                c["v_evac"] = P.now("dve")

            def emit_PV(c):
                P.wait("tensor", c["v_evac"])
                par = c["par"]
                if c["first_pv"]:
                    P.wait("tensor", opool.cons[par])
                pt = c["pt"]
                for j in range(4):
                    kt = c["kt0"] + j
                    first = c["first_pv"] and j == 0
                    P.op("tensor", lambda j=j, kt=kt, par=par, pt=pt,
                         first=first:
                         nc.tensor.matmul(
                             opool.t[par][:, 0:128],
                             probsT[:, pt * 512 + 128 * j:
                                    pt * 512 + 128 * (j + 1)],
                             v_sb[:, kt * 128:(kt + 1) * 128],
                             start=first, stop=False,
                             skip_group_check=True), "pe")
                probsT_cons[pt] = P.now("pe")
                if c["is_last"]:
                    c["it_rec"]["v_pv_last"] = P.now("pe")
                    pend_epi.append(c["it_rec"])

            def emit_epilogue(rec):
                par, qt, h = rec["par"], rec["qt"], rec["h"]
                w = 2 + 2 * qt
                P.wait("scalar", denc_cons[par])
                P.op("scalar", lambda par=par, w=w: nc.scalar.copy(
                    denc[:, par * 16:par * 16 + w],
                    den[:, par * 16:par * 16 + w]), "act")
                den_cons[par] = P.now("act")
                nmx_cons[par] = den_cons[par]
                P.wait("vector", den_cons[par])
                P.op("vector", lambda par=par, w=w: nc.vector.reduce_sum(
                    dsum[:, par:par + 1], denc[:, par * 16:par * 16 + w],
                    axis=AX.X), "dve")
                denc_cons[par] = P.now("dve")
                dve_gap()
                P.op("vector", lambda par=par: nc.vector.reciprocal(
                    drec[:, par:par + 1], dsum[:, par:par + 1]), "dve")
                dve_gap()
                P.wait("vector", rec["v_pv_last"])
                P.wait("vector", attn_dma[qt % 2])
                P.op("vector", lambda par=par, qt=qt, h=h:
                     nc.vector.tensor_scalar(
                         attn_tm[:, (qt % 2) * D + h * 128:
                                 (qt % 2) * D + h * 128 + 128],
                         opool.t[par][:, 0:128], drec[:, par:par + 1],
                         None, ALU.mult), "dve")
                dve_gap()
                opool.cons[par] = P.now("dve")
                qh_cons[par] = rec["v_s_last"]
                if h == HEADS - 1:
                    P.wait("sync", P.now("dve"))
                    P.dma("sync", attn_d[qt],
                          attn_tm[:, (qt % 2) * D:(qt % 2) * D + D], "dst")
                    attn_dma[qt % 2] = P.now("dst")

            epi_count = [0]

            def drain(n_keepT, n_keepPV):
                while len(pendT) > n_keepT:
                    c = pendT.pop(0)
                    emit_T(c)
                    pendPV.append(c)
                while len(pendPV) > n_keepPV:
                    emit_PV(pendPV.pop(0))
                while pend_epi:
                    emit_epilogue(pend_epi.pop(0))
                    epi_count[0] += 1

            def flush_to_epi(n):
                while epi_count[0] < n and (pendT or pendPV):
                    if pendPV:
                        emit_PV(pendPV.pop(0))
                    elif pendT:
                        c = pendT.pop(0)
                        emit_T(c)
                        pendPV.append(c)
                    while pend_epi:
                        emit_epilogue(pend_epi.pop(0))
                        epi_count[0] += 1

            def emit_qhT(it):
                qt, h = divmod(it, HEADS)
                qt, h = it // HEADS, it % HEADS
                par = it % 2
                qps = qapool.get(P, 0)
                P.op("tensor", lambda qt=qt, h=h, qps=qps:
                     nc.tensor.matmul(
                         qps[:, 0:128],
                         q_tm[:, qt * D + h * 128:qt * D + h * 128 + 128],
                         idb_sb[:], is_transpose=True, start=True,
                         stop=True), "pe")
                v_t = P.now("pe")
                P.wait("vector", v_t)
                P.wait("vector", qh_cons[par])
                P.op("vector", lambda par=par, qps=qps:
                     nc.vector.tensor_copy(
                         qh[:, par * 128:(par + 1) * 128], qps[:, 0:128]),
                     "dve")
                dve_gap()
                qapool.cons[0] = P.now("dve")
                return P.now("dve")

            NIT = NTT * HEADS
            v_qh_next = emit_qhT(0)
            for it in range(NIT):
                qt, h = it // HEADS, it % HEADS
                par = it % 2
                rec = dict(par=par, qt=qt, h=h)
                v_qh = v_qh_next
                chunks = []
                b0 = dict(kind="b0", it=it, par=par, qt=qt, den_col=0,
                          kt0=qt * 8, v_qh=v_qh, it_rec=rec)
                b1 = dict(kind="b1", it=it, par=par, qt=qt, den_col=1,
                          kt0=qt * 8 + 4, v_qh=v_qh, it_rec=rec)
                chunks = [b0, b1] + [
                    dict(kind="g", it=it, par=par, qt=qt, g=g, den_col=2 + g,
                         kt0=g * 4, v_qh=v_qh, it_rec=rec)
                    for g in range(2 * qt)]
                chunks[0]["first_pv"] = True
                for c in chunks[1:]:
                    c["first_pv"] = False
                for c in chunks:
                    c["is_last"] = False
                chunks[-1]["is_last"] = True

                flush_to_epi(it - 1)
                emit_S(b0)
                emit_S(b1)
                if it + 1 < NIT:
                    v_qh_next = emit_qhT(it + 1)
                P.wait("vector", sbnd_cons[par])
                for h2, c in enumerate((b0, b1)):
                    P.wait("vector", c["v_s"])
                    P.wait("vector", v_mask)
                    P.op("vector", lambda c=c, h2=h2, par=par:
                         nc.vector.tensor_tensor(
                             sbnd[:, par * 1024 + h2 * 512:
                                  par * 1024 + h2 * 512 + 512],
                             spool.t[c["slot"]][:],
                             mask_sb[:, h2 * 512:(h2 + 1) * 512],
                             ALU.add), "dve")
                    spool.cons[c["slot"]] = P.now("dve")
                P.wait("vector", nmx_cons[par])
                P.op("vector", lambda par=par: nc.vector.reduce_max(
                    nmx[:, par:par + 1],
                    sbnd[:, par * 1024:par * 1024 + 1024],
                    axis=AX.X, negate=True), "dve")
                dve_gap()
                v_nm = P.now("dve")
                b0["v_nm"] = v_nm
                b1["v_nm"] = v_nm
                rec["v_nm"] = v_nm
                emit_exp(b0)
                pendT.append(b0)
                drain(3, 2)
                emit_exp(b1)
                sbnd_cons[par] = b1["v_exp"]
                pendT.append(b1)
                drain(3, 2)
                for c in chunks[2:]:
                    c["v_nm"] = v_nm
                    emit_S(c)
                    emit_exp(c)
                    pendT.append(c)
                    drain(3, 2)
                rec["v_s_last"] = chunks[-1]["v_s"]
            drain(0, 0)
            tap("d_kfm", k_fm[:, 0:1024], v_kv_ready)
            tap("d_v", v_sb[:, 0:1024], v_kv_ready)
            tap("d_attn", attn_tm[:, 0:1024], P.now("dve"))
            attn_store_v = P.now("dst")
            P.barrier()
            p3.close()
        else:
            attn_store_v = None

        if stage >= 4:
            p4 = outer.enter_context(contextlib.ExitStack())
            stats = make_stats(p4, "st4")
            ln3_h = sb(p4, "ln3_h", [128, NDC * 512], BF16)
            attn_h = sb(p4, "attn_h", [128, NDC * 512], BF16)
            atb = sb(p4, "atb4", [128, 2048], BF16)
            xaT = sb(p4, "xaT4", [128, 1024], BF16)
            bfwb = sb(p4, "bfwb4", [128, 1024], BF16)
            lnpool = PsumPool(nc, p4, "lnps4", 2, [128, 1024], BF16)
            xapool = PsumPool(nc, p4, "xaps4", 2, [128, 512])
            pool = PsumPool(nc, p4, "p4ps", 4, [128, 512])
            xh = [U1f[:, i * D:(i + 1) * D] for i in range(4)]

            out_store_v = None
            for half in range(2):
                tts = [half * 4 + i for i in range(4)]
                P.wait("sync", attn_store_v)
                P.wait("sync", P.now("pe"))
                attn_flat = attn_d.rearrange("t p c -> (t p) c")
                for dvc in range(NDC):
                    P.dma("sync", attn_h[:, dvc * 512:(dvc + 1) * 512],
                          attn_flat[half * 512:(half + 1) * 512,
                                    dvc * 128:(dvc + 1) * 128],
                          "dw", transpose=True)
                v_attnh = P.now("dw")
                xv = []
                for i, tokt in enumerate(tts):
                    P.wait("sync", x1_store_v)
                    P.wait("sync", out_store_v)
                    P.dma("sync", xh[i], x1_d[tokt], "dw")
                    xv.append(P.now("dw"))

                def wo_evac(ti, s, psum, v_pe, xv=xv):
                    P.wait("vector", v_pe)
                    P.wait("vector", xv[ti])
                    P.op("vector", lambda ti=ti, s=s, psum=psum:
                         nc.vector.tensor_tensor(
                             xh[ti][:, s * SLAB:(s + 1) * SLAB],
                             xh[ti][:, s * SLAB:(s + 1) * SLAB],
                             psum[:, 0:SLAB], ALU.add), "dve")
                    return P.now("dve")

                emit_proj(f"woh{half}", attn_h[:], v_attnh, wo_d,
                          "direct", tts, pool, wo_evac)

                if DBG and half == 1:
                    tap("d_x2", xh[3], pool.cons[3])
                if DBG:
                    for i, tokt in enumerate(tts):
                        P.wait("sync", pool.cons[i])
                        P.dma("sync",
                              dbg_t["d_x2f"].rearrange(
                                  "(tt p) d -> tt p d", p=128)[tokt],
                              xh[i], "dst")
                else:
                    for i in range(4):
                        P.wait("sync", pool.cons[i])
                    P.wait("sync", P.now("dve"))
                g_pe4 = P.now("pe")
                v_ln3 = None
                for i in range(4):
                    v_ln3 = emit_ln(xh[i], pool.cons[i], i, ln3_h[:], True,
                                    stats, lnpool, lh_guard=g_pe4)
                v_xa = emit_xa(atb, xaT, xapool, ln3_h[:], v_ln3,
                               xa_guard=g_pe4)
                free_bfw = [None, None]
                slab_done = [None] * 4

                def ffn3_evac(ti, s, psum, v_pe):
                    P.wait("vector", v_pe)
                    P.op("vector", lambda ti=ti, s=s, psum=psum:
                         nc.vector.tensor_tensor(
                             xh[ti][:, s * SLAB:(s + 1) * SLAB],
                             xh[ti][:, s * SLAB:(s + 1) * SLAB],
                             psum[:, 0:SLAB], ALU.add), "dve")
                    slab_done[ti] = P.now("dve")
                    return slab_done[ti]

                emit_proj(f"ffn3h{half}", ln3_h[:], v_xa, wT_bf,
                          cast_slab["wT"], tts, pool, ffn3_evac,
                          extras=mk_ffn_extras(xaT, bfwb, free_bfw,
                                               lambda v=v_xa: v),
                          pre_slab=mk_pre_slab(bfwb, free_bfw))

                for i, tokt in enumerate(tts):
                    P.wait("sync", slab_done[i])
                    P.dma("sync",
                          out_d.rearrange("(tt p) d -> tt p d", p=128)[tokt],
                          xh[i], "dst")
                out_store_v = P.now("dst")
            p4.close()

        P.wait("sync", P.now("dst"))
        P.wait("sync", P.now("dw"))
        P.wait("sync", P.now("dg"))

        block = outer.enter_context(nc.Block())
        P.emit(block)

    return nc


def _host_inputs(inputs):
    x = np.asarray(inputs["L0_attn_x"], dtype=np.float32).reshape(8192, 4096)
    W = np.asarray(inputs["L0_ffn_W"], dtype=np.float32)
    A = np.asarray(inputs["L0_ffn_A"], dtype=np.float32).reshape(ER, D)
    Bf = np.asarray(inputs["L0_ffn_Bf"], dtype=np.float32)
    wT = np.ascontiguousarray(W.T)
    aT = np.ascontiguousarray(A.T)
    bfp = np.ascontiguousarray(Bf.transpose(0, 2, 1).reshape(ER, D))
    wkv = np.ascontiguousarray(np.concatenate(
        [np.asarray(inputs["L0_attn_wk"], dtype=np.float32),
         np.asarray(inputs["L0_attn_wv"], dtype=np.float32)], axis=1))
    wq = np.ascontiguousarray(np.asarray(inputs["L0_attn_wq"], dtype=np.float32))
    wo = np.ascontiguousarray(np.asarray(inputs["L0_attn_wo"], dtype=np.float32))
    bias = np.asarray(inputs["L0_ffn_bias"], dtype=np.float32).reshape(1, D)
    coef = np.asarray(inputs["L0_ffn_coeffs"], dtype=np.float32).reshape(1, 16)
    scal = np.asarray(inputs["L0_ffn_scales"], dtype=np.float32).reshape(1, 16)

    xt = x.reshape(64, 128, 4096)
    tri = np.triu(np.full((128, 128), NEG, np.float32), k=1)
    in_maps = []
    for r in range(N_CORES):
        xs = np.ascontiguousarray(
            xt[[r + 8 * m for m in range(8)]].reshape(TOK, D))
        m8 = np.zeros((128, 1024), np.float32)
        for p in range(8):
            if p == r:
                m8[:, p * 128:(p + 1) * 128] = tri
            elif p > r:
                m8[:, p * 128:(p + 1) * 128] = NEG
        in_maps.append({
            "x_shard": xs, "wT": wT, "wq": wq, "wo": wo, "wkv": wkv,
            "aT": aT, "bfp": bfp, "bias": bias, "coeffs": coef,
            "scales": scal, "mask8": m8,
        })
    return in_maps


def kernel(**inputs):
    in_maps = _host_inputs(inputs)
    nc = build_program()
    res = run_bass_kernel_spmd(nc, in_maps, core_ids=list(range(N_CORES)))
    out = np.zeros((64, 128, 4096), np.float32)
    for r in range(N_CORES):
        shard = np.asarray(res.results[r]["out_shard"],
                           dtype=np.float32).reshape(8, 128, 4096)
        for m in range(8):
            out[r + 8 * m] = shard[m]
    return out.reshape(1, 8192, 4096)


# version 09 (final)
# speedup vs baseline: 1.1138x; 1.1138x over previous
import math
import contextlib
import numpy as np
import concourse.bass as bass
import concourse.mybir as mybir
from concourse.bass_utils import run_bass_kernel_spmd

F32 = mybir.dt.float32
BF16 = mybir.dt.bfloat16
AF = mybir.ActivationFunctionType
ALU = mybir.AluOpType
AX = mybir.AxisListType

N_CORES = 8
TOK = 1024
NTT = 8
D = 4096
NDC = 32
SLAB = 256
NSLAB = D // SLAB
HEADS = 32
ER = 256
NEG = -30000.0
QSCALE = 1.0 / math.sqrt(128.0)
EPS = 1e-5

ENGS = ("sync", "gpsimd", "vector", "scalar", "tensor")
STAGE = 6
DBG = False


class Prog:

    def __init__(self, nc, stack):
        self.nc = nc
        self.steps = {e: [] for e in ENGS}
        self.cnt = {}
        self.h = {}
        self.lastw = {e: {} for e in ENGS}
        self.stack = stack
        self.nbar = 0

    def sem(self, name):
        self.h[name] = self.stack.enter_context(self.nc.semaphore(name))
        self.cnt[name] = 0
        return name

    def op(self, eng, fn, sem=None, inc=1):
        if sem is not None:
            h = self.h[sem]
            self.steps[eng].append(
                lambda fn=fn, h=h, inc=inc: fn().then_inc(h, inc))
            self.cnt[sem] += inc
            return (sem, self.cnt[sem])
        self.steps[eng].append(fn)
        return None

    def dma(self, eng, out, in_, sem, **kw):
        e = getattr(self.nc, eng)
        return self.op(eng, lambda e=e, out=out, in_=in_, kw=kw:
                       e.dma_start(out=out, in_=in_, **kw), sem, 16)

    def wait(self, eng, tok):
        if tok is None:
            return
        sem, val = tok
        if val <= 0 or self.lastw[eng].get(sem, -1) >= val:
            return
        self.lastw[eng][sem] = val
        h = self.h[sem]
        e = getattr(self.nc, eng)
        self.steps[eng].append(lambda e=e, h=h, val=val: e.wait_ge(h, val))

    def now(self, sem):
        return (sem, self.cnt[sem])

    def barrier(self):
        name = f"bar{self.nbar}"
        self.nbar += 1
        self.sem(name)
        h = self.h[name]
        for e in ENGS:
            eng = getattr(self.nc, e)
            self.steps[e].append(lambda eng=eng, h=h: eng.nop().then_inc(h, 1))
        self.cnt[name] = len(ENGS)
        for e in ENGS:
            self.wait(e, (name, len(ENGS)))

    def emit(self, block):
        def mk(e):
            def section(_):
                for fn in self.steps[e]:
                    fn()
            return section
        block.sync(mk("sync"))
        block.gpsimd(mk("gpsimd"))
        block.vector(mk("vector"))
        block.scalar(mk("scalar"))
        block.tensor(mk("tensor"))


class PsumPool:

    def __init__(self, nc, stack, name, n, shape, dtype=F32):
        self.t = [stack.enter_context(nc.psum_tensor(f"{name}{i}", shape, dtype))
                  for i in range(n)]
        self.cons = [None] * n

    def get(self, P, i):
        P.wait("tensor", self.cons[i])
        return self.t[i]


def build_program(stage=STAGE):
    nc = bass.Bass("TRN2", target_bir_lowering=False)

    def din(name, shape):
        return nc.dram_tensor(name, shape, F32, kind="ExternalInput").ap()

    x_d = din("x_shard", [TOK, D])
    wT_d = din("wT", [D, D])
    wq_d = din("wq", [D, D])
    wo_d = din("wo", [D, D])
    wkv_d = din("wkv", [D, 256])
    aT_d = din("aT", [D, ER])
    bfp_d = din("bfp", [ER, D])
    bias_d = din("bias", [1, D])
    coef_d = din("coeffs", [1, 16])
    scal_d = din("scales", [1, 16])
    mask_d = din("mask8", [128, 1024])
    out_d = nc.dram_tensor("out_shard", [TOK, D], F32, kind="ExternalOutput").ap()
    dbg_t = {}
    if DBG:
        for nm, shape in [("d_x1", [128, D]), ("d_ln1", [128, 512 * 4]),
                          ("d_xaT", [128, 1024]), ("d_q", [128, D]),
                          ("d_kfm", [128, 1024]), ("d_v", [128, 1024]),
                          ("d_attn", [128, 1024]), ("d_x2", [128, D]),
                          ("d_den", [128, 32]), ("d_wexp", [128, 2]),
                          ("d_stats", [128, 8]), ("d_x2f", [TOK, D])]:
            dbg_t[nm] = nc.dram_tensor(nm, shape, F32, kind="ExternalOutput").ap()

    wT_bf = nc.dram_tensor("wT_bf", [NSLAB, NDC, 128, SLAB], BF16).ap()
    wq_bf = nc.dram_tensor("wq_bf", [NSLAB, NDC, 128, SLAB], BF16).ap()
    wo_bf = nc.dram_tensor("wo_bf", [NSLAB, NDC, 128, SLAB], BF16).ap()
    aT_bf = nc.dram_tensor("aT_bf", [NDC, 128, ER], BF16).ap()
    bfp_bf = nc.dram_tensor("bfp_bf", [2, 128, D], BF16).ap()
    wkv_bf = nc.dram_tensor("wkv_bf", [NDC, 128, 256], BF16).ap()
    x1_d = nc.dram_tensor("x1_save", [NTT, 128, D], F32).ap()
    kv_in = nc.dram_tensor("kv_in", [TOK, 256], BF16)
    kv_out = nc.dram_tensor("kv_out", [N_CORES * TOK, 256], BF16)

    with contextlib.ExitStack() as outer:
        P = Prog(nc, outer)
        for s in ["dw", "dg", "dst", "pe", "act", "dve", "gp", "cc"]:
            P.sem(s)

        def sb(st, name, shape, dtype):
            return st.enter_context(nc.sbuf_tensor(name, shape, dtype))

        tapbuf = sb(outer, "tapbuf", [128, D], F32) if DBG else None

        def tap(nm, src_ap, after=None):
            if not DBG:
                return
            ncols = src_ap.shape[-1]
            P.wait("vector", after)
            P.wait("vector", P.now("dst"))
            P.op("vector", lambda src_ap=src_ap, ncols=ncols:
                 nc.vector.tensor_copy(tapbuf[:, 0:ncols], src_ap), "dve")
            P.wait("sync", P.now("dve"))
            P.dma("sync", dbg_t[nm][:, :], tapbuf[:, 0:ncols], "dst")

        U1 = sb(outer, "U1", [128, 32768], BF16)
        U2 = sb(outer, "U2", [128, 16384], BF16)
        idb_sb = sb(outer, "idb", [128, 128], BF16)
        halfcol_sb = sb(outer, "halfcol", [1, 128], BF16)
        eps_sb = sb(outer, "eps", [128, 1], F32)
        fence_sb = sb(outer, "fence", [128, 1], F32)
        bias_sb = sb(outer, "biasr", [1, D], BF16)
        wexp_sb = sb(outer, "wexp", [128, 2], F32)
        U1f = U1.ap().bitcast(F32)

        prep = outer.enter_context(contextlib.ExitStack())
        onesb_sb = sb(prep, "onesb", [128, 128], BF16)
        sel_sb = sb(prep, "sel", [16, 256], F32)
        wrow_sb = sb(prep, "wrow", [1, 16], F32)
        srow_sb = sb(prep, "srow", [1, 16], F32)
        idf16_sb = sb(prep, "idf16", [16, 16], F32)
        wcol_sb = sb(prep, "wcol", [16, 1], F32)
        wcol_ps = prep.enter_context(nc.psum_tensor("wcolps", [16, 16], F32))
        wexp_ps = prep.enter_context(nc.psum_tensor("wexpps", [128, 2], F32))

        P.op("gpsimd", lambda: nc.gpsimd.memset(onesb_sb[:], 1.0), "gp")
        P.op("gpsimd", lambda: nc.gpsimd.affine_select(
            idb_sb[:], onesb_sb[:], pattern=[[1, 128]], base=0,
            channel_multiplier=-1, compare_op=ALU.is_equal, fill=0.0), "gp")
        P.op("gpsimd", lambda: nc.gpsimd.memset(halfcol_sb[:], 0.5), "gp")
        P.op("gpsimd", lambda: nc.gpsimd.memset(eps_sb[:], EPS), "gp")
        P.op("gpsimd", lambda: nc.gpsimd.memset(sel_sb[:], 1.0), "gp")
        P.op("gpsimd", lambda: nc.gpsimd.affine_select(
            sel_sb[:], sel_sb[:], pattern=[[1, 256]], base=0,
            channel_multiplier=-16, compare_op=ALU.is_ge, fill=0.0), "gp")
        P.op("gpsimd", lambda: nc.gpsimd.affine_select(
            sel_sb[:], sel_sb[:], pattern=[[-1, 256]], base=15,
            channel_multiplier=16, compare_op=ALU.is_ge, fill=0.0), "gp")
        P.op("gpsimd", lambda: nc.gpsimd.memset(idf16_sb[:], 1.0), "gp")
        P.op("gpsimd", lambda: nc.gpsimd.affine_select(
            idf16_sb[:], idf16_sb[:], pattern=[[1, 16]], base=0,
            channel_multiplier=-1, compare_op=ALU.is_equal, fill=0.0), "gp")
        v_gp_prep = P.now("gp")

        P.dma("sync", wrow_sb[:], coef_d[:, :], "dw")
        P.dma("sync", srow_sb[:], scal_d[:, :], "dw")
        v_cs = P.now("dw")
        P.wait("vector", v_cs)
        P.op("vector", lambda: nc.vector.tensor_tensor(
            srow_sb[:], wrow_sb[:], srow_sb[:], ALU.mult), "dve")
        v_wrow = P.now("dve")
        P.wait("tensor", v_wrow)
        P.wait("tensor", v_gp_prep)
        P.op("tensor", lambda: nc.tensor.transpose(
            wcol_ps[:, 0:1], srow_sb[:], idf16_sb[0:1, 0:1]), "pe")
        v1 = P.now("pe")
        P.wait("vector", v1)
        P.op("vector", lambda: nc.vector.tensor_copy(
            wcol_sb[:], wcol_ps[:, 0:1]), "dve")
        v2 = P.now("dve")
        P.wait("tensor", v2)
        for erc in range(2):
            P.op("tensor", lambda erc=erc: nc.tensor.matmul(
                wexp_ps[:, erc:erc + 1], sel_sb[:, 128 * erc:128 * (erc + 1)],
                wcol_sb[:], start=True, stop=True), "pe")
        v3 = P.now("pe")
        P.wait("vector", v3)
        P.op("vector", lambda: nc.vector.tensor_copy(wexp_sb[:], wexp_ps[:]),
             "dve")
        v_wexp = P.now("dve")

        P.dma("gpsimd", bias_sb[:], bias_d[:, :], "dg")
        v_bias = P.now("dg")

        cast_done = {}
        cast_slab = {}
        P.dma("gpsimd", aT_bf[:], aT_d.rearrange("(dc p) c -> dc p c", p=128),
              "dg")
        cast_done["aT"] = P.now("dg")
        P.dma("gpsimd", bfp_bf[:], bfp_d.rearrange("(e p) c -> e p c", p=128),
              "dg")
        cast_done["bfp"] = P.now("dg")
        P.dma("gpsimd", wkv_bf[:], wkv_d.rearrange("(dc p) c -> dc p c", p=128),
              "dg")
        cast_done["wkv"] = P.now("dg")
        for nm, src, dst in [("wT", wT_d, wT_bf)]:
            cast_slab[nm] = []
            for s in range(NSLAB):
                P.dma("gpsimd", dst[s],
                      src[:, s * SLAB:(s + 1) * SLAB].rearrange(
                          "(dc p) c -> dc p c", p=128), "dg")
                cast_slab[nm].append(P.now("dg"))
            cast_done[nm] = P.now("dg")

        def dve_gap():
            P.op("vector", lambda: nc.vector.tensor_copy(
                fence_sb[:], eps_sb[:]), "dve")

        def act_fence():
            return P.op("scalar", lambda: nc.scalar.copy(
                fence_sb[:], eps_sb[:]), "act")

        def make_stats(st, tag):
            return [dict(
                s1=sb(st, f"{tag}s1{i}", [128, 1], F32),
                s2=sb(st, f"{tag}s2{i}", [128, 1], F32),
                var=sb(st, f"{tag}var{i}", [128, 1], F32),
                std=sb(st, f"{tag}std{i}", [128, 1], F32),
                lntm=sb(st, f"{tag}lntm{i}", [128, D], BF16),
                cons_dve=None, cons_pe=None)
                for i in range(1 if DBG else 2)]

        def emit_ln(x_ap, x_ready, ti, ln_h, half_scale, stats2, lnpool,
                    lh_guard=None, dbg_snap=False):
            stats = stats2[ti % len(stats2)]
            s1, s2, var, std, lntm = (stats[k] for k in
                                      ("s1", "s2", "var", "std", "lntm"))
            P.wait("scalar", stats["cons_dve"])
            P.wait("scalar", stats["cons_pe"])
            P.wait("scalar", x_ready)
            P.op("scalar", lambda: nc.scalar.square(lntm[:], x_ap), "act")
            P.wait("vector", P.now("act"))
            P.wait("vector", x_ready)
            P.op("vector", lambda: nc.vector.reduce_sum(
                s1[:], x_ap, axis=AX.X), "dve")
            P.op("vector", lambda: nc.vector.reduce_sum(
                s2[:], lntm[:], axis=AX.X), "dve")
            dve_gap()
            P.op("vector", lambda: nc.vector.tensor_scalar(
                s1[:], s1[:], 1.0 / D, None, ALU.mult), "dve")
            P.op("vector", lambda: nc.vector.tensor_scalar(
                s2[:], s2[:], 1.0 / D, None, ALU.mult), "dve")
            dve_gap()
            P.op("vector", lambda: nc.vector.tensor_tensor(
                var[:], s1[:], s1[:], ALU.mult), "dve")
            dve_gap()
            P.op("vector", lambda: nc.vector.tensor_tensor(
                std[:], s2[:], var[:], ALU.subtract), "dve")
            dve_gap()
            P.wait("scalar", P.now("dve"))
            P.op("scalar", lambda: nc.scalar.activation(
                var[:], std[:], AF.Sqrt, bias=eps_sb[:]), "act")
            act_fence()
            P.wait("vector", P.now("act"))
            P.op("vector", lambda: nc.vector.reciprocal(std[:], var[:]), "dve")
            dve_gap()
            if half_scale:
                P.op("vector", lambda: nc.vector.tensor_scalar(
                    std[:], std[:], 0.5, None, ALU.mult), "dve")
                dve_gap()
            if dbg_snap and DBG:
                for ci, t_ in enumerate((s1, s2, var, std)):
                    P.op("vector", lambda t_=t_, ci=ci: nc.vector.tensor_copy(
                        tapbuf[:, 3000 + ci:3001 + ci], t_[:]), "dve")
            P.op("vector", lambda: nc.vector.tensor_scalar(
                lntm[:], x_ap, s1[:], std[:], ALU.subtract, ALU.mult), "dve")
            if dbg_snap and DBG:
                P.op("vector", lambda: nc.vector.tensor_copy(
                    tapbuf[:, 3004:3008], lntm[:, 0:4]), "dve")
                P.wait("sync", P.now("dve"))
                P.dma("sync", dbg_t["d_stats"][:, :], tapbuf[:, 3000:3008],
                      "dst")
            stats["cons_dve"] = P.now("dve")
            P.wait("tensor", P.now("dve"))
            P.wait("vector", lh_guard)
            last = None
            for grp in range(4):
                bank = lnpool.get(P, grp % 2)
                for j in range(8):
                    dc = grp * 8 + j
                    P.op("tensor", lambda dc=dc, bank=bank, j=j:
                         nc.tensor.matmul(
                             bank[:, 128 * j:128 * (j + 1)],
                             lntm[:, 128 * dc:128 * (dc + 1)],
                             idb_sb[:], is_transpose=True, start=(j == 0),
                             stop=(j == 7), skip_group_check=True), "pe")
                P.wait("vector", P.now("pe"))
                dst = ln_h.rearrange("p (dc t) -> p dc t", t=512)[
                    :, grp * 8:(grp + 1) * 8, ti * 128:(ti + 1) * 128]
                P.op("vector", lambda bank=bank, dst=dst:
                     nc.vector.tensor_copy(
                         dst, bank[:].rearrange("p (a b) -> p a b", b=128)),
                     "dve")
                last = P.now("dve")
                lnpool.cons[grp % 2] = last
            stats["cons_pe"] = P.now("pe")
            return last

        def emit_proj(name, lhs_h, lhs_ready, w_dram, w_cast_v, tts,
                      pool, evac_fn, extras=None, pre_slab=None):
            wbuf = [U2.ap()[:, i * 8192:(i + 1) * 8192] for i in range(2)]
            free_v = [P.now("pe"), P.now("pe")]
            direct = w_cast_v == "direct"
            for si in range(NSLAB):
                s = si
                par = si % 2
                if direct:
                    P.wait("gpsimd", free_v[par])
                    P.dma("gpsimd",
                          wbuf[par].rearrange("p (dc c) -> p dc c", c=SLAB),
                          w_dram[:, s * SLAB:(s + 1) * SLAB].rearrange(
                              "(dc p) c -> p dc c", p=128), "dg")
                    v_load = P.now("dg")
                else:
                    P.wait("sync", w_cast_v[s]
                           if isinstance(w_cast_v, list) else w_cast_v)
                    P.wait("sync", free_v[par])
                    P.dma("sync",
                          wbuf[par].rearrange("p (dc c) -> p dc c", c=SLAB),
                          w_dram[s].rearrange("dc p c -> p dc c"), "dw")
                    v_load = P.now("dw")
                extra_tok = pre_slab(s, par) if pre_slab is not None else None
                P.wait("tensor", v_load)
                P.wait("tensor", extra_tok)
                P.wait("tensor", lhs_ready)
                for ti in range(len(tts)):
                    psum = pool.get(P, ti)
                    for dc in range(NDC):
                        is_last = (dc == NDC - 1) and extras is None
                        P.op("tensor",
                             lambda dc=dc, ti=ti, par=par, psum=psum,
                             is_last=is_last:
                             nc.tensor.matmul(
                                 psum[:, 0:SLAB],
                                 lhs_h[:, dc * 512 + ti * 128:
                                       dc * 512 + ti * 128 + 128],
                                 wbuf[par][:, dc * SLAB:(dc + 1) * SLAB],
                                 start=(dc == 0), stop=is_last), "pe")
                    if extras is not None:
                        extras(s, ti, psum, par)
                    v_pe = P.now("pe")
                    pool.cons[ti] = evac_fn(ti, s, psum, v_pe)
                free_v[par] = P.now("pe")

        atb_cons = [None, None]

        def emit_xa(st_atb, xaT, xapool, ln_h, v_ln, xa_guard=None):
            for erc in range(2):
                for g in range(8):
                    gpar = g % 2
                    P.wait("sync", cast_done["aT"])
                    P.wait("sync", atb_cons[gpar])
                    P.dma("sync",
                          st_atb[:, gpar * 1024:gpar * 1024 + 1024].rearrange(
                              "p (a c) -> p a c", c=ER),
                          aT_bf[g * 4:(g + 1) * 4].rearrange(
                              "a p c -> p a c"), "dw")
                    v_atb = P.now("dw")
                    P.wait("tensor", v_atb)
                    P.wait("tensor", v_ln)
                    if g == 0:
                        P.wait("tensor", xapool.cons[erc])
                    for j in range(4):
                        dc = g * 4 + j
                        P.op("tensor", lambda erc=erc, dc=dc, j=j, gpar=gpar:
                             nc.tensor.matmul(
                                 xapool.t[erc][:],
                                 st_atb[:, gpar * 1024 + j * ER + erc * 128:
                                        gpar * 1024 + j * ER + erc * 128 + 128],
                                 ln_h[:, dc * 512:dc * 512 + 512],
                                 start=(dc == 0), stop=(dc == NDC - 1)), "pe")
                    atb_cons[gpar] = P.now("pe")
            v_xaps = P.now("pe")
            P.wait("vector", v_xaps)
            P.wait("vector", xa_guard)
            P.wait("vector", v_wexp)
            for erc in range(2):
                P.op("vector", lambda erc=erc: nc.vector.tensor_scalar(
                    xaT[:, erc * 512:(erc + 1) * 512], xapool.t[erc][:],
                    wexp_sb[:, erc:erc + 1], None, ALU.mult), "dve")
                xapool.cons[erc] = P.now("dve")
            return P.now("dve")

        def mk_ffn_extras(xaT, bfwb, free_bfw, get_vxa):
            def extras(s, ti, psum, par):
                P.wait("tensor", get_vxa())
                for erc in range(2):
                    P.op("tensor", lambda erc=erc, ti=ti, par=par, psum=psum:
                         nc.tensor.matmul(
                             psum[:, 0:SLAB],
                             xaT[:, erc * 512 + ti * 128:
                                 erc * 512 + ti * 128 + 128],
                             bfwb[:, par * 512 + erc * 256:
                                  par * 512 + erc * 256 + 256],
                             start=False, stop=False), "pe")
                P.wait("tensor", v_bias)
                P.wait("tensor", v_gp_prep)
                P.op("tensor", lambda s=s, psum=psum: nc.tensor.matmul(
                    psum[:, 0:SLAB], halfcol_sb[:],
                    bias_sb[:, s * SLAB:(s + 1) * SLAB],
                    start=False, stop=True), "pe")
                free_bfw[par] = P.now("pe")
            return extras

        def mk_pre_slab(bfwb, free_bfw):
            if free_bfw[0] is None:
                free_bfw[0] = P.now("pe")
                free_bfw[1] = P.now("pe")

            def pre_slab(s, par):
                P.wait("sync", cast_done["bfp"])
                P.wait("sync", free_bfw[par])
                for erc in range(2):
                    P.dma("sync",
                          bfwb[:, par * 512 + erc * 256:
                               par * 512 + erc * 256 + 256],
                          bfp_bf[erc][:, s * SLAB:(s + 1) * SLAB], "dw")
                return P.now("dw")
            return pre_slab

        P.barrier()
        prep.close()
        x1_store_v = None
        if stage >= 1:
            p1 = outer.enter_context(contextlib.ExitStack())
            ln1_h = sb(p1, "ln1_h", [128, NDC * 512], BF16)
            stats = make_stats(p1, "st1")
            atb = sb(p1, "atb1", [128, 2048], BF16)
            xaT = sb(p1, "xaT1", [128, 1024], BF16)
            bfwb = sb(p1, "bfwb1", [128, 1024], BF16)
            lnpool = PsumPool(nc, p1, "lnps1", 2, [128, 1024], BF16)
            xapool = PsumPool(nc, p1, "xaps1", 2, [128, 512])
            pool = PsumPool(nc, p1, "ffn1ps", 4, [128, 512])
            xh = [U1f[:, i * D:(i + 1) * D] for i in range(4)]

            for half in range(2):
                tts = [half * 4 + i for i in range(4)]
                xv = []
                for i, tokt in enumerate(tts):
                    P.wait("sync", x1_store_v)
                    P.dma("sync", xh[i],
                          x_d.rearrange("(tt p) d -> tt p d", p=128)[tokt],
                          "dw")
                    xv.append(P.now("dw"))
                g_pe = P.now("pe")
                v_ln1 = None
                for i in range(4):
                    v_ln1 = emit_ln(xh[i], xv[i], i, ln1_h[:], True,
                                    stats, lnpool, lh_guard=g_pe,
                                    dbg_snap=(half == 0 and i == 0))
                v_xa = emit_xa(atb, xaT, xapool, ln1_h[:], v_ln1, xa_guard=g_pe)
                free_bfw = [None, None]

                def ffn_evac(ti, s, psum, v_pe, xv=xv):
                    P.wait("vector", v_pe)
                    P.wait("vector", xv[ti])
                    P.op("vector", lambda ti=ti, s=s, psum=psum:
                         nc.vector.tensor_tensor(
                             xh[ti][:, s * SLAB:(s + 1) * SLAB],
                             xh[ti][:, s * SLAB:(s + 1) * SLAB],
                             psum[:, 0:SLAB], ALU.add), "dve")
                    return P.now("dve")

                emit_proj(f"ffn1h{half}", ln1_h[:], v_ln1, wT_bf,
                          cast_slab["wT"], tts, pool, ffn_evac,
                          extras=mk_ffn_extras(xaT, bfwb, free_bfw,
                                               lambda v=v_xa: v),
                          pre_slab=mk_pre_slab(bfwb, free_bfw))

                for i, tokt in enumerate(tts):
                    P.wait("sync", pool.cons[i])
                    P.dma("sync", x1_d[tokt], xh[i], "dst")
                x1_store_v = P.now("dst")
                if half == 0:
                    tap("d_x1", xh[0], pool.cons[0])
                    tap("d_ln1", ln1_h[:, 0:2048], v_ln1)
                    tap("d_xaT", xaT[:], v_xa)
                    tap("d_wexp", wexp_sb[:], v_wexp)
            P.barrier()
            p1.close()

        s23 = outer.enter_context(contextlib.ExitStack())
        q_tm = U1.ap()
        v_cc = None
        if stage >= 2:
            p2 = s23.enter_context(contextlib.ExitStack())
            ln2_h = sb(p2, "ln2_h", [128, NDC * 512], BF16)
            stats = make_stats(p2, "st2")
            nx1t = 1 if DBG else 2
            x1t = sb(p2, "x1t", [128, D * 2 * nx1t], BF16)
            x1tf = x1t.ap().bitcast(F32)
            kvw = sb(p2, "kvw", [128, NDC * 256], BF16)
            kvt = sb(p2, "kvt", [128, 512], BF16)
            lnpool = PsumPool(nc, p2, "lnps2", 2, [128, 1024], BF16)
            qpool = PsumPool(nc, p2, "qps", 4, [128, 512])
            kvpool = PsumPool(nc, p2, "kvps", 2, [128, 512])

            P.wait("sync", cast_done["wkv"])
            P.dma("sync", kvw[:].rearrange("p (dc c) -> p dc c", c=256),
                  wkv_bf[:].rearrange("dc p c -> p dc c"), "dw")
            v_kvw = P.now("dw")
            kvt_free = [None, None]
            for half in range(2):
                tts = [half * 4 + i for i in range(4)]
                g_pe2 = P.now("pe")
                v_ln2 = None
                for i, tokt in enumerate(tts):
                    st_i = stats[i % len(stats)]
                    P.wait("sync", st_i["cons_dve"])
                    P.wait("sync", st_i["cons_pe"])
                    if nx1t == 1:
                        P.wait("sync", P.now("act"))
                        P.wait("sync", P.now("dve"))
                    P.wait("sync", x1_store_v)
                    xsl = x1tf[:, (i % nx1t) * D:(i % nx1t) * D + D]
                    P.dma("sync", xsl, x1_d[tokt], "dw")
                    v_x1 = P.now("dw")
                    v_ln2 = emit_ln(xsl, v_x1, i, ln2_h[:], False,
                                    stats, lnpool, lh_guard=g_pe2)

                P.wait("tensor", v_kvw)
                for ti in range(4):
                    tokt = half * 4 + ti
                    par = ti % 2
                    psum = kvpool.get(P, par)
                    for dc in range(NDC):
                        P.op("tensor", lambda dc=dc, ti=ti, psum=psum:
                             nc.tensor.matmul(
                                 psum[:, 0:256],
                                 ln2_h[:, dc * 512 + ti * 128:
                                       dc * 512 + ti * 128 + 128],
                                 kvw[:, dc * 256:(dc + 1) * 256],
                                 start=(dc == 0), stop=(dc == NDC - 1)), "pe")
                    v_pe = P.now("pe")
                    P.wait("vector", v_pe)
                    P.wait("vector", kvt_free[par])
                    P.op("vector", lambda par=par, psum=psum:
                         nc.vector.tensor_copy(
                             kvt[:, par * 256:(par + 1) * 256],
                             psum[:, 0:256]), "dve")
                    dve_gap()
                    v_e = P.now("dve")
                    kvpool.cons[par] = v_e
                    P.wait("gpsimd", v_e)
                    P.dma("gpsimd",
                          kv_in.ap().rearrange("(tt p) c -> tt p c", p=128)[tokt],
                          kvt[:, par * 256:(par + 1) * 256], "dg")
                    kvt_free[par] = P.now("dg")
                if half == 1:
                    v_kv_dma = P.now("dg")
                    P.wait("gpsimd", v_kv_dma)
                    P.op("gpsimd", lambda: nc.gpsimd.collective_compute(
                        "AllGather", ALU.bypass,
                        replica_groups=[list(range(N_CORES))],
                        ins=[kv_in.ap().opt()], outs=[kv_out.ap().opt()]),
                        "cc")
                    v_cc = P.now("cc")
                def q_evac(ti, s, psum, v_pe, half=half):
                    tokt = half * 4 + ti
                    P.wait("scalar", v_pe)
                    P.op("scalar", lambda tokt=tokt, s=s, psum=psum:
                         nc.scalar.activation(
                             q_tm[:, tokt * D + s * SLAB:
                                  tokt * D + (s + 1) * SLAB],
                             psum[:, 0:SLAB], AF.Copy, scale=QSCALE), "act")
                    return P.now("act")

                emit_proj(f"qprojh{half}", ln2_h[:], v_ln2, wq_d,
                          "direct", tts, qpool, q_evac)

            tap("d_q", q_tm[:, 0:D].bitcast(F32).bitcast(BF16) if False else q_tm[:, 0:D], P.now("act"))
            P.barrier()
            p2.close()

        attn_d = nc.dram_tensor("attn_save", [NTT, 128, D], BF16).ap()
        if stage >= 3:
            p3 = s23.enter_context(contextlib.ExitStack())
            attn_tm = sb(p3, "attn_tm", [128, 2 * D], BF16)
            k_fm = U2.ap()[:, 0:8192]
            v_sb = U2.ap()[:, 8192:16384]
            mask_sb = sb(p3, "mask", [128, 1024], F32)
            ktmp = sb(p3, "ktmp", [128, 2048], BF16)
            sbnd = sb(p3, "sbnd", [128, 2048], F32)
            qh = sb(p3, "qh", [128, 256], BF16)
            nmx = sb(p3, "nmx", [128, 2], F32)
            den = sb(p3, "den", [128, 32], F32)
            dsum = sb(p3, "dsum", [128, 2], F32)
            denc = sb(p3, "denc", [128, 32], F32)
            drec = sb(p3, "drec", [128, 2], F32)
            probs = sb(p3, "probs", [128, 2048], BF16)
            probsT = sb(p3, "probsT", [128, 2048], BF16)
            kstack = p3.enter_context(contextlib.ExitStack())
            kpool = PsumPool(nc, kstack, "kps", 2, [128, 1024], BF16)

            P.dma("sync", mask_sb[:], mask_d[:, :], "dw")
            v_mask = P.now("dw")

            P.wait("sync", v_cc)
            kv_re3 = kv_out.ap().rearrange("(r m p) c -> r m p c", p=128, m=8)
            ktmp_free = [None, None]
            for g in range(8):
                P.dma("sync", v_sb[:, g * 1024:(g + 1) * 1024].rearrange(
                    "p (t c) -> p t c", c=128),
                    kv_re3[:, g, :, 128:256].rearrange(
                        "t p c -> p t c"), "dw")
                par = g % 2
                P.wait("sync", ktmp_free[par])
                P.dma("sync",
                      ktmp[:, par * 1024:(par + 1) * 1024].rearrange(
                          "p (t c) -> p t c", c=128),
                      kv_re3[:, g, :, 0:128].rearrange(
                          "t p c -> p t c"), "dw")
                v_l = P.now("dw")
                P.wait("tensor", v_l)
                bank = kpool.get(P, par)
                for j in range(8):
                    P.op("tensor", lambda j=j, bank=bank, par=par:
                         nc.tensor.matmul(
                             bank[:, 128 * j:128 * (j + 1)],
                             ktmp[:, par * 1024 + 128 * j:
                                  par * 1024 + 128 * (j + 1)],
                             idb_sb[:], is_transpose=True, start=(j == 0),
                             stop=(j == 7), skip_group_check=True), "pe")
                ktmp_free[par] = P.now("pe")
                P.wait("vector", ktmp_free[par])
                P.op("vector", lambda g=g, bank=bank: nc.vector.tensor_copy(
                    k_fm[:, g * 1024:(g + 1) * 1024], bank[:]), "dve")
                kpool.cons[par] = P.now("dve")
            v_kv_ready = P.now("dve")
            P.wait("tensor", v_kv_ready)
            P.wait("tensor", P.now("dw"))
            kstack.close()
            spool = PsumPool(nc, p3, "sps", 3, [128, 512])
            tpool = PsumPool(nc, p3, "tps", 2, [128, 1024], BF16)
            opool = PsumPool(nc, p3, "ops", 2, [128, 512])
            qapool = PsumPool(nc, p3, "qaps", 1, [128, 1024], BF16)

            probs_cons = [None, None, None, None]
            probsT_cons = [None, None, None, None]
            sbnd_cons = [None, None]
            den_cons = [None, None]
            denc_cons = [None, None]
            nmx_cons = [None, None]
            qh_cons = [None, None]
            attn_dma = [None, None]
            sps_i = [0]
            tps_i = [0]
            pbs_i = [0]
            pts_i = [0]
            pendT = []
            pendPV = []
            pend_epi = []

            def emit_S(c):
                a = sps_i[0] % 3
                sps_i[0] += 1
                c["slot"] = a
                psum = spool.get(P, a)
                P.wait("tensor", c["v_qh"])
                if c["kind"] == "g":
                    k0 = c["g"] * 512
                else:
                    k0 = c["qt"] * 1024 + (0 if c["kind"] == "b0" else 512)
                par = c["par"]
                P.op("tensor", lambda psum=psum, par=par, k0=k0:
                     nc.tensor.matmul(
                         psum[:], qh[:, par * 128:(par + 1) * 128],
                         k_fm[:, k0:k0 + 512], start=True, stop=True), "pe")
                c["v_s"] = P.now("pe")

            def emit_exp(c):
                pb = pbs_i[0] % 4
                pbs_i[0] += 1
                c["pb"] = pb
                par = c["par"]
                P.wait("scalar", probs_cons[pb])
                P.wait("scalar", den_cons[par])
                if c["kind"] == "g":
                    P.wait("scalar", c["v_s"])
                    P.wait("scalar", c["v_nm"])
                    src = spool.t[c["slot"]][:]
                else:
                    P.wait("scalar", c["v_nm"])
                    h2 = 0 if c["kind"] == "b0" else 1
                    src = sbnd[:, par * 1024 + h2 * 512:
                               par * 1024 + h2 * 512 + 512]
                dcol = c["den_col"]
                P.op("scalar", lambda src=src, pb=pb, par=par, dcol=dcol:
                     nc.scalar.activation(
                         probs[:, pb * 512:(pb + 1) * 512], src, AF.Exp,
                         bias=nmx[:, par:par + 1],
                         accum_out=den[:, par * 16 + dcol:
                                       par * 16 + dcol + 1]), "act")
                c["v_exp"] = P.now("act")
                if c["kind"] == "g":
                    spool.cons[c["slot"]] = c["v_exp"]

            def emit_T(c):
                b = tps_i[0] % 2
                tps_i[0] += 1
                bank = tpool.get(P, b)
                P.wait("tensor", c["v_exp"])
                pb = c["pb"]
                for j in range(4):
                    P.op("tensor", lambda j=j, bank=bank, pb=pb:
                         nc.tensor.matmul(
                             bank[:, 128 * j:128 * (j + 1)],
                             probs[:, pb * 512 + 128 * j:
                                   pb * 512 + 128 * (j + 1)],
                             idb_sb[:], is_transpose=True,
                             start=(j == 0), stop=(j == 3),
                             skip_group_check=True), "pe")
                probs_cons[pb] = P.now("pe")
                pt = pts_i[0] % 4
                pts_i[0] += 1
                c["pt"] = pt
                P.wait("vector", P.now("pe"))
                P.wait("vector", probsT_cons[pt])
                P.op("vector", lambda b=b, pt=pt: nc.vector.tensor_copy(
                    probsT[:, pt * 512:(pt + 1) * 512],
                    tpool.t[b][:, 0:512]), "dve")
                tpool.cons[b] = P.now("dve")
                c["v_evac"] = P.now("dve")

            def emit_PV(c):
                P.wait("tensor", c["v_evac"])
                par = c["par"]
                if c["first_pv"]:
                    P.wait("tensor", opool.cons[par])
                pt = c["pt"]
                for j in range(4):
                    kt = c["kt0"] + j
                    first = c["first_pv"] and j == 0
                    P.op("tensor", lambda j=j, kt=kt, par=par, pt=pt,
                         first=first:
                         nc.tensor.matmul(
                             opool.t[par][:, 0:128],
                             probsT[:, pt * 512 + 128 * j:
                                    pt * 512 + 128 * (j + 1)],
                             v_sb[:, kt * 128:(kt + 1) * 128],
                             start=first, stop=False,
                             skip_group_check=True), "pe")
                probsT_cons[pt] = P.now("pe")
                if c["is_last"]:
                    c["it_rec"]["v_pv_last"] = P.now("pe")
                    pend_epi.append(c["it_rec"])

            def emit_epilogue(rec):
                par, qt, h = rec["par"], rec["qt"], rec["h"]
                w = 2 + 2 * qt
                P.wait("scalar", denc_cons[par])
                P.op("scalar", lambda par=par, w=w: nc.scalar.copy(
                    denc[:, par * 16:par * 16 + w],
                    den[:, par * 16:par * 16 + w]), "act")
                den_cons[par] = P.now("act")
                nmx_cons[par] = den_cons[par]
                P.wait("vector", den_cons[par])
                P.op("vector", lambda par=par, w=w: nc.vector.reduce_sum(
                    dsum[:, par:par + 1], denc[:, par * 16:par * 16 + w],
                    axis=AX.X), "dve")
                denc_cons[par] = P.now("dve")
                dve_gap()
                P.op("vector", lambda par=par: nc.vector.reciprocal(
                    drec[:, par:par + 1], dsum[:, par:par + 1]), "dve")
                dve_gap()
                P.wait("vector", rec["v_pv_last"])
                P.wait("vector", attn_dma[qt % 2])
                P.op("vector", lambda par=par, qt=qt, h=h:
                     nc.vector.tensor_scalar(
                         attn_tm[:, (qt % 2) * D + h * 128:
                                 (qt % 2) * D + h * 128 + 128],
                         opool.t[par][:, 0:128], drec[:, par:par + 1],
                         None, ALU.mult), "dve")
                dve_gap()
                opool.cons[par] = P.now("dve")
                qh_cons[par] = rec["v_s_last"]
                if h == HEADS - 1:
                    P.wait("sync", P.now("dve"))
                    P.dma("sync", attn_d[qt],
                          attn_tm[:, (qt % 2) * D:(qt % 2) * D + D], "dst")
                    attn_dma[qt % 2] = P.now("dst")

            it_list = []
            for m in range(4):
                for h_ in range(HEADS):
                    it_list.append((m, h_))
                    it_list.append((7 - m, h_))
            epi_count = [0]

            def drain(n_keepT, n_keepPV):
                while len(pendT) > n_keepT:
                    c = pendT.pop(0)
                    emit_T(c)
                    pendPV.append(c)
                while len(pendPV) > n_keepPV:
                    emit_PV(pendPV.pop(0))
                while pend_epi:
                    emit_epilogue(pend_epi.pop(0))
                    epi_count[0] += 1

            def flush_to_epi(n):
                while epi_count[0] < n and (pendT or pendPV):
                    if pendPV:
                        emit_PV(pendPV.pop(0))
                    elif pendT:
                        c = pendT.pop(0)
                        emit_T(c)
                        pendPV.append(c)
                    while pend_epi:
                        emit_epilogue(pend_epi.pop(0))
                        epi_count[0] += 1

            def emit_qhT(it):
                qt, h = it_list[it]
                par = it % 2
                qps = qapool.get(P, 0)
                P.op("tensor", lambda qt=qt, h=h, qps=qps:
                     nc.tensor.matmul(
                         qps[:, 0:128],
                         q_tm[:, qt * D + h * 128:qt * D + h * 128 + 128],
                         idb_sb[:], is_transpose=True, start=True,
                         stop=True), "pe")
                v_t = P.now("pe")
                P.wait("vector", v_t)
                P.wait("vector", qh_cons[par])
                P.op("vector", lambda par=par, qps=qps:
                     nc.vector.tensor_copy(
                         qh[:, par * 128:(par + 1) * 128], qps[:, 0:128]),
                     "dve")
                dve_gap()
                qapool.cons[0] = P.now("dve")
                return P.now("dve")

            NIT = NTT * HEADS
            v_qh_next = emit_qhT(0)
            for it in range(NIT):
                qt, h = it_list[it]
                par = it % 2
                rec = dict(par=par, qt=qt, h=h)
                v_qh = v_qh_next
                chunks = []
                b0 = dict(kind="b0", it=it, par=par, qt=qt, den_col=0,
                          kt0=qt * 8, v_qh=v_qh, it_rec=rec)
                b1 = dict(kind="b1", it=it, par=par, qt=qt, den_col=1,
                          kt0=qt * 8 + 4, v_qh=v_qh, it_rec=rec)
                chunks = [b0, b1] + [
                    dict(kind="g", it=it, par=par, qt=qt, g=g, den_col=2 + g,
                         kt0=g * 4, v_qh=v_qh, it_rec=rec)
                    for g in range(2 * qt)]
                chunks[0]["first_pv"] = True
                for c in chunks[1:]:
                    c["first_pv"] = False
                for c in chunks:
                    c["is_last"] = False
                chunks[-1]["is_last"] = True

                flush_to_epi(it - 1)
                emit_S(b0)
                emit_S(b1)
                if it + 1 < NIT:
                    v_qh_next = emit_qhT(it + 1)
                P.wait("vector", sbnd_cons[par])
                for h2, c in enumerate((b0, b1)):
                    P.wait("vector", c["v_s"])
                    P.wait("vector", v_mask)
                    P.op("vector", lambda c=c, h2=h2, par=par:
                         nc.vector.tensor_tensor(
                             sbnd[:, par * 1024 + h2 * 512:
                                  par * 1024 + h2 * 512 + 512],
                             spool.t[c["slot"]][:],
                             mask_sb[:, h2 * 512:(h2 + 1) * 512],
                             ALU.add), "dve")
                    spool.cons[c["slot"]] = P.now("dve")
                P.wait("vector", nmx_cons[par])
                P.op("vector", lambda par=par: nc.vector.reduce_max(
                    nmx[:, par:par + 1],
                    sbnd[:, par * 1024:par * 1024 + 1024],
                    axis=AX.X, negate=True), "dve")
                dve_gap()
                v_nm = P.now("dve")
                b0["v_nm"] = v_nm
                b1["v_nm"] = v_nm
                rec["v_nm"] = v_nm
                emit_exp(b0)
                pendT.append(b0)
                drain(3, 2)
                emit_exp(b1)
                sbnd_cons[par] = b1["v_exp"]
                pendT.append(b1)
                drain(3, 2)
                for c in chunks[2:]:
                    c["v_nm"] = v_nm
                    emit_S(c)
                    emit_exp(c)
                    pendT.append(c)
                    drain(3, 2)
                rec["v_s_last"] = chunks[-1]["v_s"]
            drain(0, 0)
            tap("d_kfm", k_fm[:, 0:1024], v_kv_ready)
            tap("d_v", v_sb[:, 0:1024], v_kv_ready)
            tap("d_attn", attn_tm[:, 0:1024], P.now("dve"))
            attn_store_v = P.now("dst")
            P.barrier()
            p3.close()
        else:
            attn_store_v = None

        if stage >= 4:
            p4 = outer.enter_context(contextlib.ExitStack())
            stats = make_stats(p4, "st4")
            ln3_h = sb(p4, "ln3_h", [128, NDC * 512], BF16)
            attn_h = sb(p4, "attn_h", [128, NDC * 512], BF16)
            atb = sb(p4, "atb4", [128, 2048], BF16)
            xaT = sb(p4, "xaT4", [128, 1024], BF16)
            bfwb = sb(p4, "bfwb4", [128, 1024], BF16)
            lnpool = PsumPool(nc, p4, "lnps4", 2, [128, 1024], BF16)
            xapool = PsumPool(nc, p4, "xaps4", 2, [128, 512])
            pool = PsumPool(nc, p4, "p4ps", 4, [128, 512])
            xh = [U1f[:, i * D:(i + 1) * D] for i in range(4)]

            out_store_v = None
            for half in range(2):
                tts = [half * 4 + i for i in range(4)]
                P.wait("sync", attn_store_v)
                P.wait("sync", P.now("pe"))
                attn_flat = attn_d.rearrange("t p c -> (t p) c")
                for dvc in range(NDC):
                    P.dma("sync", attn_h[:, dvc * 512:(dvc + 1) * 512],
                          attn_flat[half * 512:(half + 1) * 512,
                                    dvc * 128:(dvc + 1) * 128],
                          "dw", transpose=True)
                v_attnh = P.now("dw")
                xv = []
                for i, tokt in enumerate(tts):
                    P.wait("sync", x1_store_v)
                    P.wait("sync", out_store_v)
                    P.dma("sync", xh[i], x1_d[tokt], "dw")
                    xv.append(P.now("dw"))

                def wo_evac(ti, s, psum, v_pe, xv=xv):
                    P.wait("vector", v_pe)
                    P.wait("vector", xv[ti])
                    P.op("vector", lambda ti=ti, s=s, psum=psum:
                         nc.vector.tensor_tensor(
                             xh[ti][:, s * SLAB:(s + 1) * SLAB],
                             xh[ti][:, s * SLAB:(s + 1) * SLAB],
                             psum[:, 0:SLAB], ALU.add), "dve")
                    return P.now("dve")

                emit_proj(f"woh{half}", attn_h[:], v_attnh, wo_d,
                          "direct", tts, pool, wo_evac)

                if DBG and half == 1:
                    tap("d_x2", xh[3], pool.cons[3])
                if DBG:
                    for i, tokt in enumerate(tts):
                        P.wait("sync", pool.cons[i])
                        P.dma("sync",
                              dbg_t["d_x2f"].rearrange(
                                  "(tt p) d -> tt p d", p=128)[tokt],
                              xh[i], "dst")
                else:
                    for i in range(4):
                        P.wait("sync", pool.cons[i])
                    P.wait("sync", P.now("dve"))
                g_pe4 = P.now("pe")
                v_ln3 = None
                for i in range(4):
                    v_ln3 = emit_ln(xh[i], pool.cons[i], i, ln3_h[:], True,
                                    stats, lnpool, lh_guard=g_pe4)
                v_xa = emit_xa(atb, xaT, xapool, ln3_h[:], v_ln3,
                               xa_guard=g_pe4)
                free_bfw = [None, None]
                slab_done = [None] * 4

                def ffn3_evac(ti, s, psum, v_pe):
                    P.wait("vector", v_pe)
                    P.op("vector", lambda ti=ti, s=s, psum=psum:
                         nc.vector.tensor_tensor(
                             xh[ti][:, s * SLAB:(s + 1) * SLAB],
                             xh[ti][:, s * SLAB:(s + 1) * SLAB],
                             psum[:, 0:SLAB], ALU.add), "dve")
                    slab_done[ti] = P.now("dve")
                    return slab_done[ti]

                emit_proj(f"ffn3h{half}", ln3_h[:], v_xa, wT_bf,
                          cast_slab["wT"], tts, pool, ffn3_evac,
                          extras=mk_ffn_extras(xaT, bfwb, free_bfw,
                                               lambda v=v_xa: v),
                          pre_slab=mk_pre_slab(bfwb, free_bfw))

                for i, tokt in enumerate(tts):
                    P.wait("sync", slab_done[i])
                    P.dma("sync",
                          out_d.rearrange("(tt p) d -> tt p d", p=128)[tokt],
                          xh[i], "dst")
                out_store_v = P.now("dst")
            p4.close()

        P.wait("sync", P.now("dst"))
        P.wait("sync", P.now("dw"))
        P.wait("sync", P.now("dg"))

        block = outer.enter_context(nc.Block())
        P.emit(block)

    return nc


def _host_inputs(inputs):
    x = np.asarray(inputs["L0_attn_x"], dtype=np.float32).reshape(8192, 4096)
    W = np.asarray(inputs["L0_ffn_W"], dtype=np.float32)
    A = np.asarray(inputs["L0_ffn_A"], dtype=np.float32).reshape(ER, D)
    Bf = np.asarray(inputs["L0_ffn_Bf"], dtype=np.float32)
    wT = np.ascontiguousarray(W.T)
    aT = np.ascontiguousarray(A.T)
    bfp = np.ascontiguousarray(Bf.transpose(0, 2, 1).reshape(ER, D))
    wkv = np.ascontiguousarray(np.concatenate(
        [np.asarray(inputs["L0_attn_wk"], dtype=np.float32),
         np.asarray(inputs["L0_attn_wv"], dtype=np.float32)], axis=1))
    wq = np.ascontiguousarray(np.asarray(inputs["L0_attn_wq"], dtype=np.float32))
    wo = np.ascontiguousarray(np.asarray(inputs["L0_attn_wo"], dtype=np.float32))
    bias = np.asarray(inputs["L0_ffn_bias"], dtype=np.float32).reshape(1, D)
    coef = np.asarray(inputs["L0_ffn_coeffs"], dtype=np.float32).reshape(1, 16)
    scal = np.asarray(inputs["L0_ffn_scales"], dtype=np.float32).reshape(1, 16)

    xt = x.reshape(64, 128, 4096)
    tri = np.triu(np.full((128, 128), NEG, np.float32), k=1)
    in_maps = []
    for r in range(N_CORES):
        xs = np.ascontiguousarray(
            xt[[r + 8 * m for m in range(8)]].reshape(TOK, D))
        m8 = np.zeros((128, 1024), np.float32)
        for p in range(8):
            if p == r:
                m8[:, p * 128:(p + 1) * 128] = tri
            elif p > r:
                m8[:, p * 128:(p + 1) * 128] = NEG
        in_maps.append({
            "x_shard": xs, "wT": wT, "wq": wq, "wo": wo, "wkv": wkv,
            "aT": aT, "bfp": bfp, "bias": bias, "coeffs": coef,
            "scales": scal, "mask8": m8,
        })
    return in_maps


def kernel(**inputs):
    in_maps = _host_inputs(inputs)
    nc = build_program()
    res = run_bass_kernel_spmd(nc, in_maps, core_ids=list(range(N_CORES)))
    out = np.zeros((64, 128, 4096), np.float32)
    for r in range(N_CORES):
        shard = np.asarray(res.results[r]["out_shard"],
                           dtype=np.float32).reshape(8, 128, 4096)
        for m in range(8):
            out[r + 8 * m] = shard[m]
    return out.reshape(1, 8192, 4096)
